# version 01 (final)
# baseline (speedup 1.0000x reference)
import math
import numpy as np
import ml_dtypes
import concourse.bass as bass
import concourse.mybir as mybir
from concourse.bass_utils import run_bass_kernel_spmd

F32 = mybir.dt.float32
BF16 = mybir.dt.bfloat16
OP = mybir.AluOpType
AF = mybir.ActivationFunctionType
AX = mybir.AxisListType.X

NCORES = 8
SEQS = 2
T = 2048
D = 1024
NCH = D // 128
NTC = T // 128
NTB = T // 512
INNER = 4096
NJC = INNER // 128

M32C = 1.5 * 2.0 ** 23
M64C = 1.5 * 2.0 ** 52
FP32_EPS = float(np.finfo(np.float32).eps)
LN_EPS = 1e-5
SCL = 1.0 / 32.0

RET_H = 8
_GAM = 1.0 - np.exp(np.linspace(math.log(1.0 / 32.0), math.log(1.0 / 512.0),
                                RET_H))
GAMMAS = [float(g) for g in _GAM.astype(np.float32)]

UPTO = "all"
DBG_SBUF = []


class Prog:
    ENGS = ("sync", "tensor", "vector", "scalar", "gpsimd")

    def __init__(self, nc):
        self.nc = nc
        self.streams = {e: [] for e in self.ENGS}
        self.sems = {}
        self.cnt = {}
        self.last_write = {}
        self.spacer_ap = None
        self.pe_spacer = None

    def sem(self, name):
        if name not in self.sems:
            self.sems[name] = self.nc.alloc_semaphore(name=name)
            self.cnt[name] = 0
        return self.sems[name]

    def emit(self, eng, fn, incs=()):
        hs = []
        for s, n in incs:
            h = self.sem(s)
            self.cnt[s] += n
            hs.append((h, n))
        self.streams[eng].append((fn, tuple(hs)))

    def wait(self, eng, sname, val):
        h = self.sem(sname)
        self.streams[eng].append((lambda e, h=h, v=val: e.wait_ge(h, v), ()))

    def wait_now(self, eng, sname):
        self.wait(eng, sname, self.cnt.get(sname, 0))

    def barrier(self, engs=ENGS):
        for e in engs:
            h = self.sem("bar_" + e)
            self.cnt["bar_" + e] += 1
            self.streams[e].append((lambda en, h=h: en.sem_inc(h, 1), ()))
        for e in engs:
            for o in engs:
                if o != e:
                    self.wait(e, "bar_" + o, self.cnt["bar_" + o])

    def dma(self, eng, out_ap, in_ap, extra_incs=()):
        sname = {"sync": "SD", "gpsimd": "GD", "scalar": "CD",
                 "vector": "VD"}[eng]
        self.emit(eng, lambda e, o=out_ap, i=in_ap: e.dma_start(out=o, in_=i),
                  ((sname, 16),) + tuple(extra_incs))
        return self.cnt[sname]

    def finalize(self, block):
        def run_stream(eng_obj, items):
            for fn, hs in items:
                ins = fn(eng_obj)
                for h, n in hs:
                    ins.then_inc(h, n)

        block.sync(lambda e: run_stream(e, self.streams["sync"]))
        block.tensor(lambda e: run_stream(e, self.streams["tensor"]))
        block.vector(lambda e: run_stream(e, self.streams["vector"]))
        block.scalar(lambda e: run_stream(e, self.streams["scalar"]))
        block.gpsimd(lambda e: run_stream(e, self.streams["gpsimd"]))


_READ_KEYS = ("in_", "in0", "in1", "scalar", "scalar1", "scalar2", "mask",
              "on_true", "on_false", "data")


def _ap_iv(ap):
    try:
        name = ap.tensor.name
        off = ap.offset
        span = 1
        dims = list(ap.ap)
        for step, count in dims[1:]:
            span += (int(count) - 1) * abs(int(step))
        return (name, off, off + span)
    except Exception:
        return None


def _ivs(vals):
    out = []
    for v in vals:
        if v is None or isinstance(v, (int, float, bool)):
            continue
        iv = _ap_iv(v)
        if iv is not None:
            out.append(iv)
    return out


def _overlap(reads, writes):
    for (rn, r0, r1) in reads:
        for (wn, w0, w1) in writes:
            if rn == wn and r0 < w1 and w0 < r1:
                return True
    return False


def _guard(P, eng, meth, a, kw):
    if meth == "memset":
        writes = _ivs([a[0] if a else kw.get("ap")])
        reads = []
    else:
        writes = _ivs([kw.get("out"), kw.get("accum_out")])
        reads = _ivs([kw.get(k) for k in _READ_KEYS])
    prev = P.last_write.get(eng, [])
    if _overlap(reads, prev):
        spc = P.spacer_ap
        P.streams[eng].append(
            (lambda e, o=spc[0], i=spc[1]:
             (e.tensor_copy(out=o, in_=i) if eng == "vector"
              else e.copy(out=o, in_=i)), ()))
    P.last_write[eng] = writes


def PE_FLUSH(P, incs):
    P.emit("tensor", lambda e: e.ldweights(P.pe_spacer), incs)


def MM(P, out, lhsT, rhs, start=True, stop=True, incs=(), flush=True):
    if incs and flush:
        P.emit("tensor", lambda e, o=out, l=lhsT, r=rhs, s=start, t=stop:
               e.matmul(o, l, r, start=s, stop=t), ())
        PE_FLUSH(P, incs)
    else:
        P.emit("tensor", lambda e, o=out, l=lhsT, r=rhs, s=start, t=stop:
               e.matmul(o, l, r, start=s, stop=t), incs)


def TR(P, out, in_, ident, incs=(), flush=True):
    if incs and flush:
        P.emit("tensor", lambda e, o=out, i=in_, d=ident:
               e.transpose(o, i, d), ())
        PE_FLUSH(P, incs)
    else:
        P.emit("tensor", lambda e, o=out, i=in_, d=ident:
               e.transpose(o, i, d), incs)


def _flush(P, eng, incs):
    spc = P.spacer_ap
    if eng == "vector":
        P.emit(eng, lambda e, o=spc[0], i=spc[1]:
               e.tensor_copy(out=o, in_=i), incs)
    else:
        P.emit(eng, lambda e, o=spc[2]: e.memzero(o), incs)
    P.last_write[eng] = _ivs([spc[0] if eng == "vector" else spc[2]])


def V(P, meth, *a, incs=(), **kw):
    _guard(P, "vector", meth, a, kw)
    P.emit("vector", lambda e, m=meth, a=a, kw=kw: getattr(e, m)(*a, **kw),
           ())
    if incs:
        _flush(P, "vector", incs)


def S(P, meth, *a, incs=(), **kw):
    _guard(P, "scalar", meth, a, kw)
    P.emit("scalar", lambda e, m=meth, a=a, kw=kw: getattr(e, m)(*a, **kw),
           ())
    if incs:
        _flush(P, "scalar", incs)


def host_constants():
    c = {}
    c["ident"] = np.eye(128, dtype=np.float32)
    c["ident_bf"] = np.eye(128).astype(ml_dtypes.bfloat16)
    c["ones_r"] = np.ones((1, 128), dtype=np.float32)
    c["ones127_r"] = np.full((1, 128), 1.0 / 127.0, dtype=np.float32)
    c["ones_c"] = np.ones((128, 1), dtype=np.float32)

    i = np.arange(64)
    theta = np.power(10000.0, (-2.0 * (i // 2) / 64.0)).astype(np.float32)
    pos = np.arange(T, dtype=np.float32)
    ang = pos[:, None] * theta[None, :]
    sign = np.where(np.arange(T) % 2 == 1, -1.0, 1.0).astype(np.float32)
    rf = (np.cos(ang) + sign[:, None] * np.sin(ang)).astype(np.float32)
    c["rope"] = np.ascontiguousarray(np.tile(rf.T, (2, 1)))

    indq = np.zeros((8, 128, 16), dtype=np.float32)
    for och in range(8):
        indq[och, 0:64, 2 * och] = 1.0
        indq[och, 64:128, 2 * och + 1] = 1.0
    c["indq"] = np.ascontiguousarray(indq.transpose(1, 0, 2))
    indk = np.zeros((2, 128, 4), dtype=np.float32)
    for och in range(2):
        indk[och, 0:64, 2 * och] = 1.0
        indk[och, 64:128, 2 * och + 1] = 1.0
    c["indk"] = np.ascontiguousarray(indk.transpose(1, 0, 2))
    m416 = np.zeros((4, 16), dtype=np.float32)
    for h in range(16):
        m416[h % 4, h] = 1.0
    c["map4to16"] = m416

    dt_t = np.zeros((RET_H, 128, 128), dtype=np.float32)
    gout = np.zeros((RET_H, 128, 128), dtype=np.float32)
    gst = np.zeros((128, RET_H), dtype=np.float32)
    tt = np.arange(128)
    for h, g in enumerate(GAMMAS):
        d = tt[None, :] - tt[:, None]
        dt_t[h] = np.where(d >= 0, np.power(np.float64(g), np.maximum(d, 0)),
                           0.0).astype(np.float32)
        gout[h] = np.power(np.float64(g), tt + 1)[None, :].astype(np.float32)
        gst[:, h] = np.power(np.float64(g), 127 - tt).astype(np.float32)
    c["ret_dt"] = np.ascontiguousarray(dt_t.transpose(1, 0, 2))
    c["ret_gout"] = np.ascontiguousarray(gout.transpose(1, 0, 2))
    c["ret_gst"] = gst
    return c


def _tiles(w):
    ni, no = w.shape[0] // 128, w.shape[1] // 128
    return np.ascontiguousarray(
        w.reshape(ni, 128, no, 128).transpose(0, 2, 1, 3))


def host_inputs(inputs):
    c = host_constants()
    f32 = np.float32

    def colsplit(v):
        return np.ascontiguousarray(np.asarray(v, f32).reshape(NCH, 128).T)

    sh = dict(c)
    sh["w_q"] = _tiles(np.asarray(inputs["L0_attn_w_q"], f32))
    sh["w_k"] = _tiles(np.asarray(inputs["L0_attn_w_k"], f32))
    sh["w_v"] = np.ascontiguousarray(
        np.asarray(inputs["L0_attn_w_v"], f32).reshape(NCH, 128, 256))
    sh["w_o"] = _tiles(np.asarray(inputs["L0_attn_w_o"], f32))
    sh["g_col"] = colsplit(inputs["L0_attn_g"])
    sh["wg_t"] = np.ascontiguousarray(
        np.asarray(inputs["L0_ffn_w_gate"], f32).T.reshape(NCH, 128, INNER))
    sh["wu_t"] = np.ascontiguousarray(
        np.asarray(inputs["L0_ffn_w_up"], f32).T.reshape(NCH, 128, INNER))
    sh["wd_t"] = np.ascontiguousarray(
        np.asarray(inputs["L0_ffn_w_down"], f32).T.reshape(NJC, 128, D))
    sh["l1wq"] = _tiles(np.asarray(inputs["L1_attn_Wq"], f32).T)
    sh["l1wk"] = _tiles(np.asarray(inputs["L1_attn_Wk"], f32).T)
    sh["l1wv"] = np.ascontiguousarray(
        np.asarray(inputs["L1_attn_Wv"], f32).T.reshape(NCH, 128, D))
    sh["l1wo"] = _tiles(np.asarray(inputs["L1_attn_Wo"], f32).T)
    sh["gnw_col"] = colsplit(inputs["L1_attn_gn_weight"])
    sh["gnb_col"] = colsplit(inputs["L1_attn_gn_bias"])
    sh["fnw_col"] = colsplit(inputs["L1_ffn_norm_w"])
    sh["fnb_col"] = colsplit(inputs["L1_ffn_norm_b"])
    sh["fq_t"] = _tiles(np.asarray(inputs["L1_ffn_q_w"], f32).T)
    sh["fk_t"] = _tiles(np.asarray(inputs["L1_ffn_k_w"], f32).T)
    sh["fv_mov"] = np.ascontiguousarray(
        np.asarray(inputs["L1_ffn_v_w"], f32).T.reshape(NCH, 128, D))
    sh["fo_t"] = _tiles(np.asarray(inputs["L1_ffn_o_w"], f32).T)
    sh["fqb_col"] = colsplit(inputs["L1_ffn_q_b"])
    sh["fkb_col"] = colsplit(inputs["L1_ffn_k_b"])
    sh["fob_col"] = colsplit(inputs["L1_ffn_o_b"])
    sh["fvb_bc"] = np.ascontiguousarray(
        np.broadcast_to(np.asarray(inputs["L1_ffn_v_b"], f32)[None, :],
                        (128, D)))

    x = np.asarray(inputs["L0_attn_x"], f32)
    maps = []
    for core in range(NCORES):
        m = dict(sh)
        xs = x[core * SEQS:(core + 1) * SEQS]
        m["x_in"] = np.ascontiguousarray(
            xs.transpose(0, 2, 1).reshape(SEQS, NCH, 128, T))
        maps.append(m)
    return maps


class Ctx:
    pass


_IN_SHAPES = [
    ("ident", (128, 128)), ("ones_r", (1, 128)), ("ones127_r", (1, 128)),
    ("ones_c", (128, 1)), ("rope", (128, T)), ("indq", (128, 8, 16)),
    ("indk", (128, 2, 4)), ("map4to16", (4, 16)),
    ("ret_dt", (128, RET_H, 128)), ("ret_gout", (128, RET_H, 128)),
    ("ret_gst", (128, RET_H)),
    ("w_q", (NCH, NCH, 128, 128)), ("w_k", (NCH, 2, 128, 128)),
    ("w_v", (NCH, 128, 256)), ("w_o", (NCH, NCH, 128, 128)),
    ("g_col", (128, NCH)),
    ("wg_t", (NCH, 128, INNER)), ("wu_t", (NCH, 128, INNER)),
    ("wd_t", (NJC, 128, D)),
    ("l1wq", (NCH, NCH, 128, 128)), ("l1wk", (NCH, NCH, 128, 128)),
    ("l1wv", (NCH, 128, D)), ("l1wo", (NCH, NCH, 128, 128)),
    ("gnw_col", (128, NCH)), ("gnb_col", (128, NCH)),
    ("fnw_col", (128, NCH)), ("fnb_col", (128, NCH)),
    ("fq_t", (NCH, NCH, 128, 128)), ("fk_t", (NCH, NCH, 128, 128)),
    ("fv_mov", (NCH, 128, D)), ("fo_t", (NCH, NCH, 128, 128)),
    ("fqb_col", (128, NCH)), ("fkb_col", (128, NCH)),
    ("fob_col", (128, NCH)), ("fvb_bc", (128, D)),
]


def declare(nc):
    cx = Ctx()
    dt = {}
    dt["x_in"] = nc.dram_tensor("x_in", [SEQS, NCH, 128, T], F32,
                                kind="ExternalInput").ap()
    for n, s in _IN_SHAPES:
        dt[n] = nc.dram_tensor(n, list(s), F32, kind="ExternalInput").ap()
    dt["ident_bf"] = nc.dram_tensor("ident_bf", [128, 128], BF16,
                                    kind="ExternalInput").ap()
    dt["out"] = nc.dram_tensor("out", [SEQS, NCH, 128, T], F32,
                               kind="ExternalOutput").ap()
    for n, s, d in [("tern_g", (NCH, 128, INNER), BF16),
                    ("tern_u", (NCH, 128, INNER), BF16),
                    ("tern_d", (NJC, 128, D), BF16),
                    ("q_scr", (NCH, 128, T), F32),
                    ("k_scr", (NCH, 128, T), F32),
                    ("v_scr", (NTC, 128, D), F32),
                    ("y_scr", (NCH, 128, T), F32)]:
        dt[n] = nc.dram_tensor(n, list(s), d, kind="Internal").ap()
    cx.d = dt

    sb = {}

    def salloc(name, shape, dty=F32):
        sb[name] = nc.alloc_sbuf_tensor("sb_" + name, list(shape), dty).ap()

    salloc("x", (128, NCH * T))
    salloc("xq", (128, NCH * T), BF16)
    salloc("P2a", (128, 4096))
    salloc("P2b", (128, 4096))
    salloc("P2c", (128, 4096))
    salloc("P2d", (128, 4096))
    salloc("P4", (128, 4352))
    salloc("ident", (128, 128))
    salloc("ident_bf", (128, 128), BF16)
    salloc("ones_r", (1, 128))
    salloc("ones127_r", (1, 128))
    salloc("ones_c", (128, 1))
    salloc("indq", (128, 8 * 16))
    salloc("indk", (128, 2 * 4))
    salloc("map4to16", (4, 16))
    salloc("ret_gst", (128, RET_H))
    salloc("g_col", (128, NCH))
    salloc("gnw_col", (128, NCH))
    salloc("gnb_col", (128, NCH))
    salloc("fnw_col", (128, NCH))
    salloc("fnb_col", (128, NCH))
    salloc("fqb_col", (128, NCH))
    salloc("fkb_col", (128, NCH))
    salloc("fob_col", (128, NCH))
    salloc("stat", (128, NTC * 12))
    salloc("rows", (1, T))
    salloc("statT", (128, 128))
    salloc("pd", (128, 128))
    salloc("bias16", (128, 16))
    salloc("bias8", (128, 8))
    salloc("small", (128, 64))
    salloc("nrow", (1, 128))
    salloc("recip_r", (1, 512))
    salloc("evac", (128, 512))
    salloc("evac2", (128, 512))
    salloc("sqb", (128, 512))
    salloc("xn_t", (128, 512))
    salloc("wtile", (128, 4 * 128))
    salloc("colsc", (128, NTC * 2))
    salloc("mh_col", (128, NTC))
    salloc("S", (128, RET_H * 128))
    salloc("gn_sum", (128, RET_H))
    salloc("gn_sq", (128, RET_H))
    salloc("gn_cols", (128, 16))
    salloc("wsc", (128, 8))
    salloc("spc", (128, 4))
    salloc("onecol", (128, 1))
    salloc("delta", (128, 1))
    salloc("w0c", (128, 1))
    cx.s = sb

    cx.ps = [nc.alloc_psum_tensor(f"ps{i}", [128, 512], F32).ap()
             for i in range(8)]
    return cx


def xs(cx, ich, c0=0, c1=T):
    return cx.s["x"][:, ich * T + c0: ich * T + c1]


def xqs(cx, ich, c0=0, c1=T):
    return cx.s["xq"][:, ich * T + c0: ich * T + c1]


def load_consts(P, cx):
    d, s = cx.d, cx.s
    P.spacer_ap = (s["spc"][:, 0:1], s["spc"][:, 1:2], s["spc"][:, 3:4])
    P.pe_spacer = s["ident_bf"][:, 0:1]
    P.emit("vector", lambda e, a=s["spc"]: e.memset(a, 0.0))
    P.emit("scalar", lambda e, a=s["spc"][:, 2:3]: e.copy(out=a, in_=a)
           if False else e.memzero(a))
    names = ["ident", "ones_r", "ones127_r", "ones_c", "map4to16",
             "ret_gst", "g_col", "gnw_col", "gnb_col", "fnw_col", "fnb_col",
             "fqb_col", "fkb_col", "fob_col", "ident_bf"]
    for n in names:
        P.dma("sync", s[n], d[n])
    P.dma("sync", s["indq"], d["indq"].rearrange("p a b -> p (a b)"))
    P.dma("sync", s["indk"], d["indk"].rearrange("p a b -> p (a b)"))
    V(P, "memset", s["onecol"], 1.0)
    V(P, "tensor_single_scalar", out=s["delta"], in_=s["onecol"],
      scalar=M64C, op=OP.add)
    V(P, "tensor_single_scalar", out=s["delta"], in_=s["delta"],
      scalar=-M64C, op=OP.add)
    V(P, "tensor_single_scalar", out=s["w0c"], in_=s["delta"],
      scalar=-1.0, op=OP.mult)
    V(P, "tensor_single_scalar", out=s["w0c"], in_=s["w0c"],
      scalar=1.0, op=OP.add, incs=(("VE", 1),))
    P.barrier()


def magic_round(P, cx, dst_ap, src_ap, tmp1_ap, tmp2_ap):
    s = cx.s
    V(P, "tensor_single_scalar", out=tmp1_ap, in_=src_ap, scalar=M32C,
      op=OP.add)
    V(P, "tensor_single_scalar", out=tmp1_ap, in_=tmp1_ap, scalar=-M32C,
      op=OP.add)
    V(P, "tensor_single_scalar", out=tmp2_ap, in_=src_ap, scalar=M64C,
      op=OP.add)
    V(P, "tensor_single_scalar", out=tmp2_ap, in_=tmp2_ap, scalar=-M64C,
      op=OP.add)
    V(P, "tensor_scalar", out=tmp2_ap, in0=tmp2_ap, scalar1=s["delta"],
      scalar2=None, op0=OP.mult)
    V(P, "scalar_tensor_tensor", out=dst_ap, in0=tmp1_ap, scalar=s["w0c"],
      in1=tmp2_ap, op0=OP.mult, op1=OP.add)


def stats_to_rows(P, cx, slot0, nstat, row0, ps):
    s = cx.s
    n16 = nstat * 16
    TR(P, ps[0:n16, 0:128], s["stat"][:, slot0 * 16: slot0 * 16 + n16],
       s["ident"], incs=(("TN", 1),))
    P.wait_now("vector", "TN")
    V(P, "tensor_copy", out=s["statT"][0:n16, 0:128], in_=ps[0:n16, 0:128],
      incs=(("VE", 1),))
    P.wait_now("gpsimd", "VE")
    for i in range(nstat):
        P.dma("gpsimd", s["rows"][0:1, (row0 + i) * T:(row0 + i + 1) * T],
              s["statT"][i * 16:(i + 1) * 16, 0:128])
    P.wait_now("sync", "GD")
    P.wait_now("tensor", "GD")
    P.wait_now("vector", "GD")


def bcast_rows(P, cx, row, out_ap, ps, blocks=NTB):
    s = cx.s
    for b in range(blocks):
        MM(P, ps[:, 0:512], s["ones_r"],
           s["rows"][0:1, row * T + b * 512: row * T + (b + 1) * 512],
           incs=(("TN", 1),))
        P.wait_now("vector", "TN")
        V(P, "tensor_copy", out=out_ap[:, b * 512:(b + 1) * 512],
          in_=ps[:, 0:512], incs=(("VE", 1),))
        P.wait_now("tensor", "VE")


def stream_tiles(P, cx, srcs, consume, depth=4, queue="sync", dtype=F32):
    buf = cx.s["wtile"] if dtype == F32 else cx.s["wtile"].bitcast(BF16)
    qs = {"sync": "SD", "gpsimd": "GD"}[queue]
    base_dma = P.cnt.get(qs, 0)
    P.wait(queue, "TN", P.cnt.get("TN", 0))
    tn_after = []
    for i, src in enumerate(srcs):
        if i >= depth:
            P.wait(queue, "TN", tn_after[i - depth])
        slot = buf[:, (i % depth) * 128:(i % depth) * 128 + 128]
        P.dma(queue, slot, src)
        P.wait("tensor", qs, base_dma + 16 * (i + 1))
        consume(i, slot)
        tn_after.append(P.cnt["TN"])
    PE_FLUSH(P, (("TN", 1),))


def l0_load_x(P, cx, seq):
    for ich in range(NCH):
        P.dma("sync", xs(cx, ich), cx.d["x_in"][seq, ich])
    P.wait_now("tensor", "SD")
    P.wait_now("vector", "SD")
    P.wait_now("scalar", "SD")


def ln_stats_pass(P, cx, sum_slot, sq_slot, ps, mode="plain",
                  mu_slot=None, inv_slot=None, max_slot=None):
    s = cx.s
    st = s["stat"]
    for ich in range(NCH):
        for tc in range(NTC):
            bank = ps[(ich * NTC + tc) % 4]
            TR(P, bank[:, 0:128], xs(cx, ich, tc * 128, tc * 128 + 128),
               s["ident"], incs=(("TN", 1),))
            tnow = P.cnt["TN"]
            P.wait("vector", "TN", tnow)
            first = ich == 0
            if mode == "absmax":
                V(P, "tensor_scalar_sub", out=s["evac"][:, 0:128],
                  in0=bank[:, 0:128],
                  scalar1=st[:, mu_slot * 16 + tc: mu_slot * 16 + tc + 1])
                V(P, "reduce_max", out=s["small"][:, 0:1],
                  in_=s["evac"][:, 0:128], axis=AX, apply_absolute_value=True)
                if first:
                    V(P, "tensor_copy",
                      out=st[:, max_slot * 16 + tc: max_slot * 16 + tc + 1],
                      in_=s["small"][:, 0:1], incs=(("VE", 1),))
                else:
                    V(P, "tensor_max",
                      out=st[:, max_slot * 16 + tc: max_slot * 16 + tc + 1],
                      in0=st[:, max_slot * 16 + tc: max_slot * 16 + tc + 1],
                      in1=s["small"][:, 0:1], incs=(("VE", 1),))
            else:
                if mode == "xn":
                    V(P, "tensor_scalar", out=s["evac"][:, 0:128],
                      in0=bank[:, 0:128],
                      scalar1=st[:, mu_slot * 16 + tc: mu_slot * 16 + tc + 1],
                      scalar2=st[:, inv_slot * 16 + tc: inv_slot * 16 + tc + 1],
                      op0=OP.subtract, op1=OP.mult)
                else:
                    V(P, "tensor_copy", out=s["evac"][:, 0:128],
                      in_=bank[:, 0:128])
                src = s["evac"][:, 0:128]
                V(P, "reduce_sum", out=s["small"][:, 0:1], in_=src, axis=AX)
                V(P, "tensor_mul", out=s["sqb"][:, 0:128], in0=src, in1=src)
                V(P, "reduce_sum", out=s["small"][:, 1:2],
                  in_=s["sqb"][:, 0:128], axis=AX)
                if first:
                    V(P, "tensor_copy",
                      out=st[:, sum_slot * 16 + tc: sum_slot * 16 + tc + 1],
                      in_=s["small"][:, 0:1])
                    V(P, "tensor_copy",
                      out=st[:, sq_slot * 16 + tc: sq_slot * 16 + tc + 1],
                      in_=s["small"][:, 1:2], incs=(("VE", 1),))
                else:
                    V(P, "tensor_add",
                      out=st[:, sum_slot * 16 + tc: sum_slot * 16 + tc + 1],
                      in0=st[:, sum_slot * 16 + tc: sum_slot * 16 + tc + 1],
                      in1=s["small"][:, 0:1])
                    V(P, "tensor_add",
                      out=st[:, sq_slot * 16 + tc: sq_slot * 16 + tc + 1],
                      in0=st[:, sq_slot * 16 + tc: sq_slot * 16 + tc + 1],
                      in1=s["small"][:, 1:2], incs=(("VE", 1),))
            P.wait_now("tensor", "VE")


def slot_ap(cx, slot, tc=None):
    if tc is None:
        return cx.s["stat"][:, slot * 16:(slot + 1) * 16]
    return cx.s["stat"][:, slot * 16 + tc: slot * 16 + tc + 1]


def l0_stats(P, cx, ps, sub="all"):
    s = cx.s
    ln_stats_pass(P, cx, 0, 1, ps, mode="plain")
    P.barrier(("tensor", "vector", "scalar"))
    V(P, "tensor_single_scalar", out=slot_ap(cx, 2), in_=slot_ap(cx, 0),
      scalar=1.0 / D, op=OP.mult)
    V(P, "tensor_single_scalar", out=slot_ap(cx, 1), in_=slot_ap(cx, 1),
      scalar=1.0 / D, op=OP.mult)
    V(P, "tensor_mul", out=s["small"][:, 2:3+15], in0=slot_ap(cx, 2),
      in1=slot_ap(cx, 2))
    V(P, "tensor_sub", out=slot_ap(cx, 1), in0=slot_ap(cx, 1),
      in1=s["small"][:, 2:18], incs=(("VE", 1),))
    V(P, "tensor_single_scalar", out=s["small"][:, 18:34],
      in_=slot_ap(cx, 1), scalar=LN_EPS, op=OP.add)
    V(P, "reciprocal", out=s["small"][:, 18:34], in_=s["small"][:, 18:34],
      incs=(("VE", 1),))
    P.wait_now("scalar", "VE")
    S(P, "activation", out=slot_ap(cx, 3), in_=s["small"][:, 18:34],
      func=AF.Sqrt, incs=(("SC", 1),))
    V(P, "tensor_mul", out=s["small"][:, 2:18], in0=s["small"][:, 18:34],
      in1=slot_ap(cx, 1))
    V(P, "tensor_single_scalar", out=s["small"][:, 2:18],
      in_=s["small"][:, 2:18], scalar=FP32_EPS, op=OP.add)
    V(P, "reciprocal", out=s["small"][:, 2:18], in_=s["small"][:, 2:18],
      incs=(("VE", 1),))
    P.wait_now("scalar", "VE")
    S(P, "activation", out=s["small"][:, 2:18], in_=s["small"][:, 2:18],
      func=AF.Sqrt, incs=(("SC", 1),))
    P.wait_now("vector", "SC")
    V(P, "tensor_mul", out=slot_ap(cx, 4), in0=slot_ap(cx, 3),
      in1=s["small"][:, 2:18], incs=(("VE", 1),))
    P.barrier(("tensor", "vector", "scalar"))
    if sub == "derive1":
        return
    ln_stats_pass(P, cx, None, None, ps, mode="absmax", mu_slot=2,
                  max_slot=5)
    P.barrier(("tensor", "vector", "scalar"))
    if sub == "pass2":
        return
    V(P, "tensor_mul", out=slot_ap(cx, 6), in0=slot_ap(cx, 5),
      in1=slot_ap(cx, 3))
    V(P, "tensor_single_scalar", out=slot_ap(cx, 6), in_=slot_ap(cx, 6),
      scalar=1e-5, op=OP.max)
    V(P, "tensor_single_scalar", out=slot_ap(cx, 8), in_=slot_ap(cx, 6),
      scalar=1.0 / 127.0, op=OP.mult, incs=(("VE", 1),))
    P.wait_now("vector", "VE")
    V(P, "reciprocal", out=slot_ap(cx, 7), in_=slot_ap(cx, 8))
    V(P, "tensor_mul", out=slot_ap(cx, 7), in0=slot_ap(cx, 7),
      in1=slot_ap(cx, 3), incs=(("VE", 1),))
    P.barrier(("tensor", "vector", "scalar", "gpsimd"))


def l0_xq(P, cx, ps):
    s = cx.s
    mu_b = s["P2c"][:, 0:2048]
    cq_b = s["P2c"][:, 2048:4096]
    for slot, dst in ((2, mu_b), (7, cq_b)):
        V(P, "tensor_copy", out=slot_ap(cx, 9), in_=slot_ap(cx, slot),
          incs=(("VE", 1),))
        P.wait_now("tensor", "VE")
        stats_to_rows(P, cx, 9, 1, 0, ps[4])
        P.barrier(("tensor", "vector", "gpsimd"))
        bcast_rows(P, cx, 0, dst, ps[0])
        P.barrier(("tensor", "vector", "gpsimd"))
    P.barrier(("tensor", "vector"))
    tbuf = s["P2a"][:, 0:2048]
    m32 = s["P2a"][:, 2048:4096]
    m64 = s["P2b"][:, 0:2048]
    for ich in range(NCH):
        V(P, "tensor_sub", out=tbuf, in0=xs(cx, ich), in1=mu_b)
        V(P, "tensor_mul", out=tbuf, in0=tbuf, in1=cq_b)
        magic_round(P, cx, xqs(cx, ich), tbuf, m32, m64)
    V(P, "memset", s["small"][:, 0:1], 0.0, incs=(("VE", 1),))
    P.barrier(("vector", "tensor", "scalar"))


def l0_attn(P, cx, ps):
    s, d = cx.s, cx.d
    V(P, "tensor_copy", out=slot_ap(cx, 9), in_=slot_ap(cx, 4),
      incs=(("VE", 1),))
    P.wait_now("tensor", "VE")
    stats_to_rows(P, cx, 9, 1, 0, ps[4])
    P.barrier(("tensor", "vector", "gpsimd"))
    mu_b = s["P2c"][:, 0:2048]
    c2_b = s["P2c"][:, 2048:4096]
    bcast_rows(P, cx, 0, c2_b, ps[0])
    rope = s["P2d"][:, 1024:3072]
    P.dma("sync", rope, d["rope"])
    wk = s["P2b"]
    for ich in range(NCH):
        for och in range(2):
            P.dma("sync", wk[:, (ich * 2 + och) * 128:(ich * 2 + och + 1) * 128],
                  d["w_k"][ich, och])
    wv = s["P2b"][:, 2048:4096]
    for ich in range(NCH):
        P.dma("sync", wv[:, ich * 256:(ich + 1) * 256], d["w_v"][ich])
    P.wait_now("tensor", "SD")
    P.wait_now("vector", "SD")
    P.barrier(("tensor", "vector", "sync"))

    k_sb = s["P2a"]
    v_aug = s["P4"][:, 0:NTC * 260]

    def xnorm_tile(ich, c0, n):
        V(P, "tensor_sub", out=s["xn_t"][:, 0:n], in0=xs(cx, ich, c0, c0 + n),
          in1=mu_b[:, c0:c0 + n])
        V(P, "tensor_mul", out=s["xn_t"][:, 0:n], in0=s["xn_t"][:, 0:n],
          in1=c2_b[:, c0:c0 + n])
        V(P, "tensor_scalar_mul", out=s["xn_t"][:, 0:n], in0=s["xn_t"][:, 0:n],
          scalar1=s["g_col"][:, ich:ich + 1], incs=(("VE", 1),))

    for tb in range(NTB):
        c0 = tb * 512
        for ich in range(NCH):
            if ich > 0:
                P.wait_now("vector", "TN")
            xnorm_tile(ich, c0, 512)
            P.wait_now("tensor", "VE")
            st, sp = ich == 0, ich == NCH - 1
            for och in range(2):
                MM(P, ps[och], wk[:, (ich * 2 + och) * 128:(ich * 2 + och + 1) * 128],
                   s["xn_t"][:, 0:512], start=st, stop=sp)
            for tc4 in range(4):
                MM(P, ps[2 + tc4][:, 0:256],
                   s["xn_t"][:, tc4 * 128:tc4 * 128 + 128], wv[:, ich * 256:(ich + 1) * 256],
                   start=st, stop=sp,
                   incs=(("TN", 1),) if tc4 == 3 else ())
        P.wait_now("vector", "TN")
        for och in range(2):
            V(P, "tensor_mul", out=k_sb[:, och * T + c0: och * T + c0 + 512],
              in0=ps[och], in1=rope[:, c0:c0 + 512], incs=(("VE", 1),))
            P.wait_now("scalar", "VE")
            if och == 1:
                P.wait_now("scalar", "TN")
            S(P, "activation", out=s["sqb"], in_=k_sb[:, och * T + c0: och * T + c0 + 512],
              func=AF.Square, incs=(("SC", 1),))
            P.wait_now("tensor", "SC")
            MM(P, ps[6][0:4, 0:512], s["indk"][:, och * 4:(och + 1) * 4],
               s["sqb"], start=(och == 0), stop=(och == 1),
               incs=(("TN", 1),))
        P.wait_now("vector", "TN")
        V(P, "reduce_max", out=s["small"][0:4, 1:2], in_=ps[6][0:4, 0:512],
          axis=AX)
        if tb == 0:
            V(P, "tensor_copy", out=s["small"][0:4, 2:3],
              in_=s["small"][0:4, 1:2])
        else:
            V(P, "tensor_max", out=s["small"][0:4, 2:3],
              in0=s["small"][0:4, 2:3], in1=s["small"][0:4, 1:2])
        for tc4 in range(4):
            tc = tb * 4 + tc4
            base = tc * 260
            V(P, "tensor_copy",
              out=v_aug[:, base:base + 260].rearrange("p (g c) -> p g c", g=4)[:, :, 0:64],
              in_=ps[2 + tc4][:, 0:256].rearrange("p (g c) -> p g c", g=4))
            V(P, "memset",
              v_aug[:, base:base + 260].rearrange("p (g c) -> p g c", g=4)[:, :, 64:65],
              1.0)
        V(P, "memset", s["small"][0:1, 0:1], 0.0, incs=(("VE", 1),))
        P.wait_now("tensor", "VE")

    V(P, "memset", s["small"][0:1, 0:1], 0.0, incs=(("VE", 1),))
    P.wait_now("scalar", "VE")
    S(P, "activation", out=s["small"][0:4, 3:4], in_=s["small"][0:4, 2:3],
      func=AF.Sqrt, incs=(("SC", 1),))
    P.wait_now("tensor", "SC")
    MM(P, ps[4][0:16, 0:1], s["map4to16"], s["small"][0:4, 3:4],
       incs=(("TN", 1),))
    P.wait_now("vector", "TN")
    V(P, "tensor_copy", out=s["small"][0:16, 4:5], in_=ps[4][0:16, 0:1],
      incs=(("VE", 1),))
    P.barrier(("tensor", "vector", "scalar", "sync"))

    q_blk = s["P2b"][:, 0:2048]
    s_blk = s["P2b"][:, 2048:4096]
    aT0 = s["P2d"][:, 0:256]
    aT1 = s["P2d"][:, 256:512]
    QB = 256
    NQB = T // QB

    for qb in range(NQB):
        c0 = qb * QB
        order = [(ich, och) for ich in range(NCH) for och in range(NCH)]
        srcs = [d["w_q"][ich, och] for ich, och in order]
        state = {"ich": -1}

        def consume(i, tile, c0=c0, state=state):
            ich, och = order[i]
            if ich != state["ich"]:
                state["ich"] = ich
                if ich > 0:
                    P.wait_now("vector", "TN")
                xnorm_tile(ich, c0, QB)
                P.wait_now("tensor", "VE")
            MM(P, ps[och][:, 0:QB], tile, s["xn_t"][:, 0:QB],
               start=(ich == 0), stop=(ich == NCH - 1), incs=(("TN", 1),),
               flush=False)

        stream_tiles(P, cx, srcs, consume)
        P.wait_now("vector", "TN")
        for och in range(NCH):
            V(P, "tensor_mul", out=q_blk[:, och * QB:(och + 1) * QB],
              in0=ps[och][:, 0:QB], in1=rope[:, c0:c0 + QB],
              incs=(("VE", 1),))
        P.wait_now("scalar", "VE")
        P.wait_now("tensor", "VE")
        for och in range(NCH):
            S(P, "activation", out=s["sqb"][:, 0:QB],
              in_=q_blk[:, och * QB:(och + 1) * QB],
              func=AF.Square, incs=(("SC", 1),))
            P.wait_now("tensor", "SC")
            MM(P, ps[0][0:16, 0:QB], s["indq"][:, och * 16:(och + 1) * 16],
               s["sqb"][:, 0:QB], start=(och == 0), stop=(och == NCH - 1),
               incs=(("TN", 1),))
            if och < NCH - 1:
                P.wait_now("scalar", "TN")
        P.wait_now("vector", "TN")
        V(P, "reduce_max", out=s["small"][0:16, 5:6], in_=ps[0][0:16, 0:QB],
          axis=AX, incs=(("VE", 1),))
        P.wait_now("scalar", "VE")
        S(P, "activation", out=s["small"][0:16, 6:7], in_=s["small"][0:16, 5:6],
          func=AF.Sqrt, incs=(("SC", 1),))
        P.wait_now("vector", "SC")
        V(P, "tensor_mul", out=s["small"][0:16, 7:8], in0=s["small"][0:16, 6:7],
          in1=s["small"][0:16, 4:5])
        V(P, "tensor_single_scalar", out=s["small"][0:16, 7:8],
          in_=s["small"][0:16, 7:8], scalar=-SCL, op=OP.mult,
          incs=(("VE", 1),))
        P.wait_now("tensor", "VE")
        TR(P, ps[1][0:1, 0:16], s["small"][0:16, 7:8],
           s["ident"][0:16, 0:16], incs=(("TN", 1),))
        P.wait_now("vector", "TN")
        V(P, "tensor_copy", out=s["recip_r"][0:1, 0:16], in_=ps[1][0:1, 0:16],
          incs=(("VE", 1),))
        P.wait_now("tensor", "VE")
        MM(P, ps[1][:, 0:16], s["ones_r"], s["recip_r"][0:1, 0:16],
           incs=(("TN", 1),))
        P.wait_now("vector", "TN")
        P.wait_now("vector", "SC")
        V(P, "tensor_copy", out=s["bias16"], in_=ps[1][:, 0:16],
          incs=(("VE", 1),))
        P.wait_now("tensor", "VE")
        P.wait_now("scalar", "VE")

        for h in range(16):
            P.wait_now("tensor", "SC")
            g = h % 4
            krow = 64 * (g % 2)
            qrow = 64 * (h % 2)
            koff = (g // 2) * T
            qv = q_blk[qrow:qrow + 64, (h // 2) * QB:(h // 2) * QB + QB]
            if qrow != krow:
                P.wait_now("vector", "TN")
                V(P, "tensor_copy", out=s["xn_t"][krow:krow + 64, 0:QB],
                  in_=qv, incs=(("VE", 1),))
                P.wait_now("tensor", "VE")
                qv = s["xn_t"][krow:krow + 64, 0:QB]
            sc_after = []
            for kc in range(NTC):
                bank = ps[kc % 2]
                if kc >= 2:
                    P.wait("tensor", "SC", sc_after[kc - 2])
                MM(P, bank[:, 0:QB],
                   k_sb[krow:krow + 64, koff + kc * 128: koff + kc * 128 + 128],
                   qv, incs=(("TN", 1),))
                P.wait_now("scalar", "TN")
                aT = aT0 if kc % 2 == 0 else aT1
                S(P, "activation", out=aT, in_=bank[:, 0:QB], func=AF.Exp,
                  bias=s["bias16"][:, h:h + 1], scale=SCL, incs=(("SC", 1),))
                sc_after.append(P.cnt["SC"])
                P.wait_now("tensor", "SC")
                MM(P, ps[2][0:65, 0:QB],
                   v_aug[:, kc * 260 + g * 65: kc * 260 + (g + 1) * 65],
                   aT, start=(kc == 0), stop=(kc == NTC - 1),
                   incs=(("TN", 1),))
            P.wait_now("vector", "TN")
            V(P, "tensor_copy", out=s["evac"][0:64, 0:QB],
              in_=ps[2][0:64, 0:QB])
            V(P, "reciprocal", out=s["recip_r"][0:1, 0:QB],
              in_=ps[2][64:65, 0:QB], incs=(("VE", 1),))
            P.wait_now("tensor", "VE")
            MM(P, ps[3][0:64, 0:QB], s["ones_r"][:, 0:64],
               s["recip_r"][0:1, 0:QB], incs=(("TN", 1),))
            P.wait_now("vector", "TN")
            V(P, "tensor_mul",
              out=s_blk[qrow:qrow + 64, (h // 2) * QB:(h // 2) * QB + QB],
              in0=s["evac"][0:64, 0:QB], in1=ps[3][0:64, 0:QB],
              incs=(("VE", 1),))
            P.wait_now("tensor", "VE")

        srcs2 = [d["w_o"][ich, och] for ich in range(NCH) for och in range(NCH)]

        def consume2b(i, tile):
            ich, och = divmod(i, NCH)
            MM(P, ps[och][:, 0:QB], tile, s_blk[:, ich * QB:(ich + 1) * QB],
               start=(ich == 0), stop=(ich == NCH - 1), incs=(("TN", 1),),
               flush=False)

        stream_tiles(P, cx, srcs2, consume2b)
        P.wait_now("vector", "TN")
        for och in range(NCH):
            V(P, "tensor_add", out=s["evac"][:, 0:QB],
              in0=s_blk[:, och * QB:(och + 1) * QB], in1=ps[och][:, 0:QB])
            V(P, "tensor_add", out=xs(cx, och, c0, c0 + QB),
              in0=xs(cx, och, c0, c0 + QB), in1=s["evac"][:, 0:QB],
              incs=(("VE", 1),))
            P.wait_now("tensor", "VE")
    P.barrier()


def l0_quant_weights(P, cx, ps):
    s, d = cx.s, cx.d
    chunk = s["P2a"]
    tbuf = s["P2b"]
    terns = s["P2c"][:, 0:2048].bitcast(BF16)
    specs = [("wg_t", NCH, INNER, "tern_g", 0, 3),
             ("wu_t", NCH, INNER, "tern_u", 1, 4),
             ("wd_t", NJC, D, "tern_d", 2, 5)]
    for wname, nch, width, tname, s_slot, m_slot in specs:
        for ich in range(nch):
            P.dma("sync", chunk[:, 0:width], d[wname][ich])
            P.wait_now("vector", "SD")
            V(P, "reduce_sum", out=s["small"][:, 9:10], in_=chunk[:, 0:width],
              axis=AX, apply_absolute_value=True)
            if ich == 0:
                V(P, "tensor_copy", out=s["small"][:, 10:11],
                  in_=s["small"][:, 9:10], incs=(("VE", 1),))
            else:
                V(P, "tensor_add", out=s["small"][:, 10:11],
                  in0=s["small"][:, 10:11], in1=s["small"][:, 9:10],
                  incs=(("VE", 1),))
            P.wait_now("sync", "VE")
        P.wait_now("tensor", "VE")
        MM(P, ps[0][0:1, 0:1], s["small"][:, 10:11], s["ones_c"],
           incs=(("TN", 1),))
        P.wait_now("vector", "TN")
        V(P, "tensor_single_scalar", out=s["recip_r"][0:1, 0:1],
          in_=ps[0][0:1, 0:1], scalar=1.0 / (4096.0 * 1024.0), op=OP.mult)
        V(P, "tensor_single_scalar", out=s["recip_r"][0:1, 0:1],
          in_=s["recip_r"][0:1, 0:1], scalar=1e-5, op=OP.max)
        V(P, "reciprocal", out=s["recip_r"][0:1, 1:2],
          in_=s["recip_r"][0:1, 0:1], incs=(("VE", 1),))
        P.wait_now("tensor", "VE")
        MM(P, ps[0][:, 0:1], s["ones_r"], s["recip_r"][0:1, 1:2],
           incs=(("TN", 1),))
        MM(P, ps[0][:, 1:2], s["ones_r"], s["recip_r"][0:1, 0:1],
           incs=(("TN", 1),))
        P.wait_now("vector", "TN")
        V(P, "tensor_copy", out=s["wsc"][:, s_slot:s_slot + 1],
          in_=ps[0][:, 0:1])
        V(P, "tensor_copy", out=s["wsc"][:, m_slot:m_slot + 1],
          in_=ps[0][:, 1:2], incs=(("VE", 1),))
        P.wait_now("sync", "VE")
        for ich in range(nch):
            P.dma("sync", chunk[:, 0:width], d[wname][ich])
            P.wait_now("vector", "SD")
            V(P, "tensor_scalar_mul", out=tbuf[:, 0:width],
              in0=chunk[:, 0:width], scalar1=s["wsc"][:, s_slot:s_slot + 1])
            V(P, "tensor_single_scalar", out=chunk[:, 0:width],
              in_=tbuf[:, 0:width], scalar=-0.5, op=OP.is_lt)
            V(P, "scalar_tensor_tensor", out=terns[:, 0:width],
              in0=tbuf[:, 0:width], scalar=0.5, in1=chunk[:, 0:width],
              op0=OP.is_gt, op1=OP.subtract, incs=(("VE", 1),))
            P.wait_now("gpsimd", "VE")
            P.dma("gpsimd", d[tname][ich], terns[:, 0:width])
            P.wait_now("sync", "GD")
            P.wait_now("vector", "GD")
    P.barrier()


def l0_ffn(P, cx, ps, first_seq):
    s, d = cx.s, cx.d
    if first_seq:
        l0_quant_weights(P, cx, ps)
    V(P, "tensor_scalar_mul", out=s["colsc"][:, 0:NTC], in0=slot_ap(cx, 8),
      scalar1=s["wsc"][:, 3:4])
    V(P, "tensor_scalar_mul", out=s["colsc"][:, NTC:2 * NTC],
      in0=slot_ap(cx, 8), scalar1=s["wsc"][:, 4:5], incs=(("VE", 1),))
    P.barrier(("vector", "tensor", "scalar", "sync", "gpsimd"))

    h0 = s["P2a"]
    h1 = s["P2b"]
    ebuf = s["P2c"][:, 0:2048]
    ebuf2 = s["P2c"][:, 2048:4096]
    hq_tm = s["P2d"][:, 2048:4096].bitcast(BF16)
    wtern0 = s["P2d"][:, 0:1024].bitcast(BF16)
    wtern1 = s["P2d"][:, 1024:2048].bitcast(BF16)
    hq_fm = s["P4"][:, 0:4096].bitcast(BF16)

    for pair in range(8):
        tcs = (2 * pair, 2 * pair + 1)
        for which, tern_name, hbufs in (("g", "tern_g", (h0, h1)),
                                        ("u", "tern_u", None)):
            for ohalf in range(2):
                ocol0 = ohalf * 2048
                tn_hist = []
                P.wait("sync", "TN", P.cnt.get("TN", 0))
                for ich in range(NCH):
                    wb = wtern0 if ich % 2 == 0 else wtern1
                    if ich >= 2:
                        P.wait("sync", "TN", tn_hist[ich - 2])
                    dma_n = P.dma("sync", wb,
                                  d[tern_name][ich][:, ocol0:ocol0 + 2048])
                    P.wait("tensor", "SD", dma_n)
                    st, sp = ich == 0, ich == NCH - 1
                    for ti, tc in enumerate(tcs):
                        for ob in range(4):
                            MM(P, ps[4 * ti + ob],
                               xqs(cx, ich, tc * 128, tc * 128 + 128),
                               wb[:, ob * 512:(ob + 1) * 512],
                               start=st, stop=sp,
                               incs=(("TN", 1),) if (ti == 1 and ob == 3)
                               else ())
                    tn_hist.append(P.cnt["TN"])
                P.wait_now("vector", "TN")
                for ti, tc in enumerate(tcs):
                    hb = (h0, h1)[ti]
                    for ob in range(4):
                        dst = hb[:, ocol0 + ob * 512: ocol0 + (ob + 1) * 512]
                        if which == "g":
                            V(P, "tensor_scalar_mul", out=dst,
                              in0=ps[4 * ti + ob],
                              scalar1=s["colsc"][:, tc:tc + 1],
                              incs=(("VE", 1),))
                        else:
                            V(P, "tensor_scalar_mul", out=s["evac"],
                              in0=ps[4 * ti + ob],
                              scalar1=s["colsc"][:, NTC + tc:NTC + tc + 1])
                            V(P, "tensor_mul", out=dst, in0=dst,
                              in1=s["evac"], incs=(("VE", 1),))
                P.wait_now("tensor", "VE")
                P.wait_now("scalar", "VE")
                if which == "g":
                    for ti in range(2):
                        hb = (h0, h1)[ti]
                        S(P, "activation", out=hb[:, ocol0:ocol0 + 2048],
                          in_=hb[:, ocol0:ocol0 + 2048], func=AF.Silu,
                          incs=(("SC", 1),))
                    P.wait_now("vector", "SC")
                    P.wait_now("tensor", "SC")
        for ti, tc in enumerate(tcs):
            hb = (h0, h1)[ti]
            V(P, "reduce_max", out=s["mh_col"][:, tc:tc + 1], in_=hb,
              axis=AX, apply_absolute_value=True)
            V(P, "tensor_single_scalar", out=s["mh_col"][:, tc:tc + 1],
              in_=s["mh_col"][:, tc:tc + 1], scalar=1e-5, op=OP.max)
            V(P, "tensor_single_scalar", out=s["small"][:, 12:13],
              in_=s["mh_col"][:, tc:tc + 1], scalar=1.0 / 127.0, op=OP.mult,
              incs=(("VE", 1),))
            P.wait_now("vector", "VE")
            V(P, "reciprocal", out=s["small"][:, 13:14],
              in_=s["small"][:, 12:13])
            for half in range(2):
                hsl = hb[:, half * 2048:(half + 1) * 2048]
                V(P, "tensor_scalar_mul", out=ebuf, in0=hsl,
                  scalar1=s["small"][:, 13:14])
                magic_round(P, cx, hq_tm[:, half * 2048:(half + 1) * 2048],
                            ebuf, ebuf2, ebuf)
            V(P, "memset", s["small"][:, 14:15], 0.0, incs=(("VE", 1),))
            P.wait_now("tensor", "VE")
            for jc in range(NJC):
                bank = ps[jc % 4].bitcast(BF16)
                TR(P, bank[:, 0:128], hq_tm[:, jc * 128:(jc + 1) * 128],
                   s["ident_bf"], incs=(("TN", 1),))
                P.wait_now("vector", "TN")
                V(P, "tensor_copy",
                  out=hq_fm[:, jc * 256 + ti * 128: jc * 256 + ti * 128 + 128],
                  in_=bank[:, 0:128], incs=(("VE", 1),))
                P.wait_now("tensor", "VE")
        V(P, "tensor_scalar_mul", out=s["small"][:, 15:17],
          in0=s["mh_col"][:, 2 * pair:2 * pair + 2],
          scalar1=s["wsc"][:, 5:6])
        V(P, "tensor_single_scalar", out=s["small"][:, 15:17],
          in_=s["small"][:, 15:17], scalar=1.0 / 127.0, op=OP.mult,
          incs=(("VE", 1),))
        P.wait_now("tensor", "VE")
        TR(P, ps[4][0:2, 0:128], s["small"][:, 15:17], s["ident"],
           incs=(("TN", 1),))
        P.wait_now("vector", "TN")
        V(P, "tensor_copy", out=s["statT"][0:2, 0:128], in_=ps[4][0:2, 0:128],
          incs=(("VE", 1),))
        P.wait_now("gpsimd", "VE")
        P.dma("gpsimd", s["recip_r"][0:1, 0:256],
              s["statT"][0:2, 0:128])
        P.wait_now("tensor", "GD")
        MM(P, ps[4][:, 0:256], s["ones_r"], s["recip_r"][0:1, 0:256],
           incs=(("TN", 1),))
        P.wait_now("vector", "TN")
        V(P, "tensor_copy", out=s["sqb"][:, 0:256], in_=ps[4][:, 0:256],
          incs=(("VE", 1),))
        P.wait_now("tensor", "VE")

        srcs3 = [d["tern_d"][jc][:, och * 128:(och + 1) * 128]
                 for och in range(NCH) for jc in range(NJC)]

        def consume3(i, tile):
            och, jc = divmod(i, NJC)
            MM(P, ps[och][:, 0:256], tile, hq_fm[:, jc * 256:(jc + 1) * 256],
               start=(jc == 0), stop=(jc == NJC - 1), incs=(("TN", 1),),
               flush=False)

        stream_tiles(P, cx, srcs3, consume3, dtype=BF16)
        P.wait_now("vector", "TN")
        c0 = pair * 256
        for och in range(NCH):
            V(P, "tensor_mul", out=s["evac"][:, 0:256],
              in0=ps[och][:, 0:256], in1=s["sqb"][:, 0:256])
            V(P, "tensor_add", out=xs(cx, och, c0, c0 + 256),
              in0=xs(cx, och, c0, c0 + 256), in1=s["evac"][:, 0:256],
              incs=(("VE", 1),))
        P.wait_now("tensor", "VE")
    P.barrier()


def l1_stats(P, cx, ps):
    s = cx.s
    ln_stats_pass(P, cx, 0, 1, ps, mode="plain")
    P.barrier(("tensor", "vector", "scalar"))
    V(P, "tensor_single_scalar", out=slot_ap(cx, 2), in_=slot_ap(cx, 0),
      scalar=1.0 / D, op=OP.mult)
    V(P, "tensor_single_scalar", out=slot_ap(cx, 1), in_=slot_ap(cx, 1),
      scalar=1.0 / D, op=OP.mult)
    V(P, "tensor_mul", out=s["small"][:, 2:18], in0=slot_ap(cx, 2),
      in1=slot_ap(cx, 2))
    V(P, "tensor_sub", out=slot_ap(cx, 1), in0=slot_ap(cx, 1),
      in1=s["small"][:, 2:18], incs=(("VE", 1),))
    V(P, "tensor_single_scalar", out=s["small"][:, 18:34],
      in_=slot_ap(cx, 1), scalar=LN_EPS, op=OP.add)
    V(P, "reciprocal", out=s["small"][:, 18:34], in_=s["small"][:, 18:34],
      incs=(("VE", 1),))
    P.wait_now("scalar", "VE")
    S(P, "activation", out=slot_ap(cx, 3), in_=s["small"][:, 18:34],
      func=AF.Sqrt, incs=(("SC", 1),))
    P.wait_now("vector", "SC")
    P.wait_now("tensor", "SC")
    ln_stats_pass(P, cx, 4, 5, ps, mode="xn", mu_slot=2, inv_slot=3)
    P.barrier(("tensor", "vector", "scalar"))
    V(P, "tensor_single_scalar", out=slot_ap(cx, 4), in_=slot_ap(cx, 4),
      scalar=1.0 / D, op=OP.mult)
    V(P, "tensor_single_scalar", out=slot_ap(cx, 5), in_=slot_ap(cx, 5),
      scalar=1.0 / D, op=OP.mult)
    V(P, "tensor_mul", out=s["small"][:, 2:18], in0=slot_ap(cx, 4),
      in1=slot_ap(cx, 4))
    V(P, "tensor_sub", out=slot_ap(cx, 5), in0=slot_ap(cx, 5),
      in1=s["small"][:, 2:18], incs=(("VE", 1),))
    V(P, "tensor_single_scalar", out=s["small"][:, 18:34],
      in_=slot_ap(cx, 5), scalar=LN_EPS, op=OP.add)
    V(P, "reciprocal", out=s["small"][:, 18:34], in_=s["small"][:, 18:34],
      incs=(("VE", 1),))
    P.wait_now("scalar", "VE")
    S(P, "activation", out=s["small"][:, 2:18], in_=s["small"][:, 18:34],
      func=AF.Sqrt, incs=(("SC", 1),))
    P.wait_now("vector", "SC")
    V(P, "tensor_mul", out=slot_ap(cx, 10), in0=slot_ap(cx, 3),
      in1=s["small"][:, 2:18])
    V(P, "tensor_mul", out=slot_ap(cx, 11), in0=slot_ap(cx, 2),
      in1=slot_ap(cx, 3))
    V(P, "tensor_add", out=slot_ap(cx, 11), in0=slot_ap(cx, 11),
      in1=slot_ap(cx, 4))
    V(P, "tensor_mul", out=slot_ap(cx, 11), in0=slot_ap(cx, 11),
      in1=s["small"][:, 2:18])
    V(P, "tensor_single_scalar", out=slot_ap(cx, 11), in_=slot_ap(cx, 11),
      scalar=-1.0, op=OP.mult)
    P.barrier(("tensor", "vector", "scalar", "gpsimd"))


def l1_ffn_qkv(P, cx, ps):
    s, d = cx.s, cx.d
    rowA_b = s["P2c"][:, 0:2048]
    rowB_b = s["P2c"][:, 2048:4096]
    for slot, dst in ((10, rowA_b), (11, rowB_b)):
        V(P, "tensor_copy", out=slot_ap(cx, 9), in_=slot_ap(cx, slot),
          incs=(("VE", 1),))
        P.wait_now("tensor", "VE")
        stats_to_rows(P, cx, 9, 1, 0, ps[4])
        P.barrier(("tensor", "vector", "gpsimd"))
        bcast_rows(P, cx, 0, dst, ps[0])
        P.barrier(("tensor", "vector", "gpsimd"))
    fvb = s["P2d"][:, 0:1024]
    P.dma("sync", fvb, cx.d["fvb_bc"])
    P.wait_now("vector", "SD")
    P.barrier(("tensor", "vector", "sync", "gpsimd", "scalar"))
    xnn = s["P2a"]

    for tb in range(NTB):
        e = tb
        nch_e = NCH >> e
        c0 = tb * 512
        for ich in range(nch_e):
            dst = xnn[:, ich * 512:(ich + 1) * 512]
            V(P, "tensor_mul", out=dst, in0=xs(cx, ich, c0, c0 + 512),
              in1=rowA_b[:, c0:c0 + 512])
            V(P, "tensor_add", out=dst, in0=dst, in1=rowB_b[:, c0:c0 + 512])
            V(P, "tensor_scalar", out=dst, in0=dst,
              scalar1=s["fnw_col"][:, ich:ich + 1],
              scalar2=s["fnb_col"][:, ich:ich + 1],
              op0=OP.mult, op1=OP.add, incs=(("VE", 1),))
        P.wait_now("tensor", "VE")

        for which, wname, bcol, scr, n2 in (
                ("q", "fq_t", "fqb_col", "q_scr", "l1qn2"),
                ("k", "fk_t", "fkb_col", "k_scr", "l1kn2")):
            srcs = [d[wname][ich, och]
                    for ich in range(nch_e) for och in range(NCH)]

            def consume(i, tile, nch_e=nch_e):
                ich, och = divmod(i, NCH)
                MM(P, ps[och], tile, xnn[:, ich * 512:(ich + 1) * 512],
                   start=(ich == 0), stop=(ich == nch_e - 1),
                   incs=(("TN", 1),), flush=False)

            stream_tiles(P, cx, srcs, consume)
            P.wait_now("vector", "TN")
            gd_hist = []
            for och in range(NCH):
                eb = s["evac"] if och % 2 == 0 else s["evac2"]
                if och >= 2:
                    P.wait("vector", "GD", gd_hist[och - 2])
                    P.wait_now("vector", "SC")
                V(P, "tensor_scalar_add", out=eb, in0=ps[och],
                  scalar1=s[bcol][:, och:och + 1], incs=(("VE", 1),))
                P.wait_now("gpsimd", "VE")
                gd_hist.append(
                    P.dma("gpsimd", d[scr][och][:, c0:c0 + 512], eb))
                P.wait_now("scalar", "VE")
                S(P, "activation", out=s["sqb"], in_=eb, func=AF.Square,
                  incs=(("SC", 1),))
                P.wait_now("tensor", "SC")
                MM(P, ps[och][0:1, 0:512], s["ones_c"], s["sqb"],
                   incs=(("TN", 1),))
                P.wait_now("vector", "TN")
                nbase = 0 if which == "q" else 32
                V(P, "reduce_max",
                  out=s["nrow"][0:1, nbase + och * 4 + tb: nbase + och * 4 + tb + 1],
                  in_=ps[och][0:1, 0:512], axis=AX, incs=(("VE", 1),))
                P.wait_now("scalar", "VE")
            P.wait_now("tensor", "VE")
            P.wait_now("gpsimd", "VE")

        vw = s["P2d"]
        sd_hist = []
        P.wait("sync", "TN", P.cnt.get("TN", 0))
        for ich in range(nch_e):
            wb = vw[:, 2048 + (ich % 2) * 1024: 2048 + (ich % 2) * 1024 + 1024]
            if ich >= 2:
                P.wait("sync", "TN", sd_hist[ich - 2])
            dn = P.dma("sync", wb, d["fv_mov"][ich])
            P.wait("tensor", "SD", dn)
            for tc4 in range(4):
                for hf in range(2):
                    MM(P, ps[2 * tc4 + hf],
                       xnn[:, ich * 512 + tc4 * 128: ich * 512 + tc4 * 128 + 128],
                       wb[:, hf * 512:(hf + 1) * 512],
                       start=(ich == 0), stop=(ich == nch_e - 1),
                       incs=(("TN", 1),) if (tc4 == 3 and hf == 1) else ())
            sd_hist.append(P.cnt["TN"])
        P.wait_now("vector", "TN")
        gd2 = []
        for tc4 in range(4):
            for hf in range(2):
                eb = s["evac"] if (2 * tc4 + hf) % 2 == 0 else s["evac2"]
                if len(gd2) >= 2:
                    P.wait("vector", "GD", gd2[-2])
                V(P, "tensor_add", out=eb, in0=ps[2 * tc4 + hf],
                  in1=fvb[:, hf * 512:(hf + 1) * 512], incs=(("VE", 1),))
                P.wait_now("gpsimd", "VE")
                gd2.append(P.dma(
                    "gpsimd", d["v_scr"][tb * 4 + tc4][:, hf * 512:(hf + 1) * 512],
                    eb))
        P.wait_now("tensor", "GD")
        P.wait_now("vector", "GD")

    nr = s["nrow"]
    V(P, "tensor_reduce", out=nr[0:1, 64:72],
      in_=nr[0:1, 0:32].rearrange("a (h b) -> a h b", b=4), axis=AX,
      op=OP.max)
    V(P, "tensor_reduce", out=nr[0:1, 72:80],
      in_=nr[0:1, 32:64].rearrange("a (h b) -> a h b", b=4), axis=AX,
      op=OP.max, incs=(("VE", 1),))
    P.wait_now("scalar", "VE")
    S(P, "activation", out=nr[0:1, 64:80], in_=nr[0:1, 64:80],
      func=AF.Sqrt, incs=(("SC", 1),))
    P.wait_now("vector", "SC")
    V(P, "tensor_mul", out=nr[0:1, 80:88], in0=nr[0:1, 64:72],
      in1=nr[0:1, 72:80])
    V(P, "tensor_single_scalar", out=nr[0:1, 80:88], in_=nr[0:1, 80:88],
      scalar=-SCL, op=OP.mult, incs=(("VE", 1),))
    P.wait_now("tensor", "VE")
    MM(P, ps[0][:, 0:8], s["ones_r"], nr[0:1, 80:88], incs=(("TN", 1),))
    P.wait_now("vector", "TN")
    V(P, "tensor_copy", out=s["bias8"], in_=ps[0][:, 0:8], incs=(("VE", 1),))
    P.barrier()


def l1_retention(P, cx, ps, seq):
    s, d = cx.s, cx.d
    P.dma("sync", s["P2d"][:, 0:1024],
          d["ret_dt"].rearrange("p h c -> p (h c)"))
    P.dma("sync", s["P2d"][:, 1024:2048],
          d["ret_gout"].rearrange("p h c -> p (h c)"))
    mu_b = s["P2c"][:, 0:2048]
    inv_b = s["P2c"][:, 2048:4096]
    for slot, dst in ((2, mu_b), (3, inv_b)):
        V(P, "tensor_copy", out=slot_ap(cx, 9), in_=slot_ap(cx, slot),
          incs=(("VE", 1),))
        P.wait_now("tensor", "VE")
        stats_to_rows(P, cx, 9, 1, 0, ps[4])
        P.barrier(("tensor", "vector", "gpsimd"))
        bcast_rows(P, cx, 0, dst, ps[0])
        P.barrier(("tensor", "vector", "gpsimd"))
    V(P, "memset", s["S"], 0.0)
    V(P, "memset", s["gn_sum"], 0.0)
    V(P, "memset", s["gn_sq"], 0.0, incs=(("VE", 1),))
    P.barrier(("tensor", "vector", "scalar", "sync", "gpsimd"))

    q_g = s["P2a"][:, 0:2048]
    k_g = s["P2a"][:, 2048:4096]
    ktm = s["P2b"][:, 0:2048]
    v_g = s["P2b"][:, 2048:4096]
    dt_t = s["P2d"][:, 0:1024]
    gout = s["P2d"][:, 1024:2048]

    def xn_grp(ich, c0, n=256):
        V(P, "tensor_sub", out=s["xn_t"][:, 0:n], in0=xs(cx, ich, c0, c0 + n),
          in1=mu_b[:, c0:c0 + n])
        V(P, "tensor_mul", out=s["xn_t"][:, 0:n], in0=s["xn_t"][:, 0:n],
          in1=inv_b[:, c0:c0 + n], incs=(("VE", 1),))

    for grp in range(8):
        c0 = grp * 256
        srcs = []
        for ich in range(NCH):
            srcs += [d["l1wq"][ich, och] for och in range(NCH)]
            srcs += [d["l1wk"][ich, och] for och in range(NCH)]
        state = {"ich": -1}

        def consume(i, tile, c0=c0, state=state):
            ich, r = divmod(i, 16)
            isq, och = (r < 8), r % 8
            if ich != state["ich"]:
                state["ich"] = ich
                if ich > 0:
                    P.wait_now("vector", "TN")
                xn_grp(ich, c0)
                P.wait_now("tensor", "VE")
            bank = ps[och // 2] if isq else ps[4 + och // 2]
            first = (ich == 0) and (och % 2 == 0)
            last = (ich == NCH - 1) and (och % 2 == 1)
            MM(P, bank[:, (och % 2) * 256:(och % 2) * 256 + 256], tile,
               s["xn_t"][:, 0:256], start=first, stop=last,
               incs=(("TN", 1),), flush=False)

        stream_tiles(P, cx, srcs, consume)
        P.wait_now("vector", "TN")
        for och in range(NCH):
            V(P, "tensor_copy", out=q_g[:, och * 256:(och + 1) * 256],
              in_=ps[och // 2][:, (och % 2) * 256:(och % 2) * 256 + 256])
            V(P, "tensor_copy", out=k_g[:, och * 256:(och + 1) * 256],
              in_=ps[4 + och // 2][:, (och % 2) * 256:(och % 2) * 256 + 256],
              incs=(("VE", 1),))
        P.wait_now("tensor", "VE")
        for och in range(NCH):
            for sub in range(2):
                bank = ps[(och * 2 + sub) % 4]
                TR(P, bank[:, 0:128],
                   k_g[:, och * 256 + sub * 128: och * 256 + sub * 128 + 128],
                   s["ident"], incs=(("TN", 1),))
                P.wait_now("vector", "TN")
                V(P, "tensor_copy",
                  out=ktm[:, sub * 1024 + och * 128: sub * 1024 + och * 128 + 128],
                  in_=bank[:, 0:128], incs=(("VE", 1),))
                P.wait_now("tensor", "VE")
        sd_hist = []
        P.wait("sync", "TN", P.cnt.get("TN", 0))
        for ich in range(NCH):
            wb = s["P2d"][:, 2048 + (ich % 2) * 1024: 2048 + (ich % 2) * 1024 + 1024]
            if ich >= 2:
                P.wait("sync", "TN", sd_hist[ich - 2])
            dn = P.dma("sync", wb, d["l1wv"][ich])
            P.wait("tensor", "SD", dn)
            if ich > 0:
                P.wait_now("vector", "TN")
            xn_grp(ich, c0)
            P.wait_now("tensor", "VE")
            for tc4 in range(2):
                for hf in range(2):
                    MM(P, ps[4 + 2 * tc4 + hf],
                       s["xn_t"][:, tc4 * 128:tc4 * 128 + 128],
                       wb[:, hf * 512:(hf + 1) * 512],
                       start=(ich == 0), stop=(ich == NCH - 1),
                       incs=(("TN", 1),) if (tc4 == 1 and hf == 1) else ())
            sd_hist.append(P.cnt["TN"])
        P.wait_now("vector", "TN")
        for tc4 in range(2):
            for hf in range(2):
                V(P, "tensor_copy",
                  out=v_g[:, tc4 * 1024 + hf * 512: tc4 * 1024 + hf * 512 + 512],
                  in_=ps[4 + 2 * tc4 + hf],
                  incs=(("VE", 1),) if (tc4 == 1 and hf == 1) else ())
        P.wait_now("tensor", "VE")

        for tc4 in range(2):
            gc0 = tc4 * 128
            for h in range(RET_H):
                qsl = q_g[:, h * 256 + gc0: h * 256 + gc0 + 128]
                ksl = k_g[:, h * 256 + gc0: h * 256 + gc0 + 128]
                ktm_sl = ktm[:, tc4 * 1024 + h * 128: tc4 * 1024 + h * 128 + 128]
                v_sl = v_g[:, tc4 * 1024 + h * 128: tc4 * 1024 + h * 128 + 128]
                S_sl = s["S"][:, h * 128:(h + 1) * 128]
                MM(P, ps[0][:, 0:128], ksl, qsl, incs=(("TN", 1),))
                P.wait_now("vector", "TN")
                V(P, "tensor_mul", out=s["pd"], in0=ps[0][:, 0:128],
                  in1=dt_t[:, h * 128:(h + 1) * 128])
                V(P, "tensor_mul", out=s["evac"][:, 0:128], in0=qsl,
                  in1=gout[:, h * 128:(h + 1) * 128])
                V(P, "tensor_scalar_mul", out=s["evac"][:, 128:256],
                  in0=ktm_sl, scalar1=s["ret_gst"][:, h:h + 1],
                  incs=(("VE", 1),))
                P.wait_now("tensor", "VE")
                MM(P, ps[1][:, 0:128], v_sl, s["pd"], start=True, stop=False)
                MM(P, ps[1][:, 0:128], S_sl, s["evac"][:, 0:128],
                   start=False, stop=True, incs=(("TN", 1),))
                MM(P, ps[2][:, 0:128], s["evac"][:, 128:256], v_sl,
                   incs=(("TN", 1),))
                P.wait_now("vector", "TN")
                V(P, "scalar_tensor_tensor", out=S_sl, in0=S_sl,
                  scalar=GAMMAS[h] ** 128, in1=ps[2][:, 0:128],
                  op0=OP.mult, op1=OP.add)
                yb = s["xn_t"][:, (tc4 % 2) * 256:(tc4 % 2) * 256 + 128]
                V(P, "tensor_copy", out=yb, in_=ps[1][:, 0:128])
                V(P, "reduce_sum", out=s["small"][:, 30:31],
                  in_=ps[1][:, 0:128], axis=AX)
                V(P, "tensor_add", out=s["gn_sum"][:, h:h + 1],
                  in0=s["gn_sum"][:, h:h + 1], in1=s["small"][:, 30:31],
                  incs=(("VE", 1),))
                P.wait_now("scalar", "VE")
                S(P, "activation", out=s["sqb"][:, 0:128], in_=ps[1][:, 0:128],
                  func=AF.Square, incs=(("SC", 1),))
                P.wait_now("vector", "SC")
                V(P, "reduce_sum", out=s["small"][:, 31:32],
                  in_=s["sqb"][:, 0:128], axis=AX)
                V(P, "tensor_add", out=s["gn_sq"][:, h:h + 1],
                  in0=s["gn_sq"][:, h:h + 1], in1=s["small"][:, 31:32],
                  incs=(("VE", 1),))
                P.wait_now("gpsimd", "VE")
                P.dma("gpsimd", d["y_scr"][h][:, c0 + gc0: c0 + gc0 + 128], yb)
                P.wait_now("vector", "GD")
                P.wait_now("tensor", "VE")
                P.wait_now("tensor", "SC")
    MM(P, ps[0][0:8, 0:1], s["gn_sum"], s["ones_c"], incs=(("TN", 1),))
    MM(P, ps[0][0:8, 1:2], s["gn_sq"], s["ones_c"], incs=(("TN", 1),))
    P.wait_now("vector", "TN")
    ntot = 1.0 / (T * 128.0)
    V(P, "tensor_single_scalar", out=s["small"][0:8, 33:34],
      in_=ps[0][0:8, 0:1], scalar=ntot, op=OP.mult)
    V(P, "tensor_single_scalar", out=s["small"][0:8, 34:35],
      in_=ps[0][0:8, 1:2], scalar=ntot, op=OP.mult)
    V(P, "tensor_mul", out=s["small"][0:8, 35:36], in0=s["small"][0:8, 33:34],
      in1=s["small"][0:8, 33:34])
    V(P, "tensor_sub", out=s["small"][0:8, 34:35], in0=s["small"][0:8, 34:35],
      in1=s["small"][0:8, 35:36], incs=(("VE", 1),))
    V(P, "tensor_single_scalar", out=s["small"][0:8, 34:35],
      in_=s["small"][0:8, 34:35], scalar=LN_EPS, op=OP.add)
    V(P, "reciprocal", out=s["small"][0:8, 34:35], in_=s["small"][0:8, 34:35],
      incs=(("VE", 1),))
    P.wait_now("scalar", "VE")
    S(P, "activation", out=s["small"][0:8, 35:36], in_=s["small"][0:8, 34:35],
      func=AF.Sqrt, incs=(("SC", 1),))
    P.wait_now("tensor", "SC")
    P.wait_now("vector", "SC")
    V(P, "tensor_copy", out=s["small"][0:8, 36:37], in_=s["small"][0:8, 33:34],
      incs=(("VE", 1),))
    P.wait_now("tensor", "VE")
    TR(P, ps[0][0:1, 0:8], s["small"][0:8, 36:37],
       s["ident"][0:8, 0:8], incs=(("TN", 1),))
    TR(P, ps[1][0:1, 0:8], s["small"][0:8, 35:36],
       s["ident"][0:8, 0:8], incs=(("TN", 1),))
    P.wait_now("vector", "TN")
    V(P, "tensor_copy", out=s["recip_r"][0:1, 0:8], in_=ps[0][0:1, 0:8])
    V(P, "tensor_copy", out=s["recip_r"][0:1, 8:16], in_=ps[1][0:1, 0:8],
      incs=(("VE", 1),))
    P.wait_now("tensor", "VE")
    MM(P, ps[0][:, 0:8], s["ones_r"], s["recip_r"][0:1, 0:8],
       incs=(("TN", 1),))
    MM(P, ps[0][:, 8:16], s["ones_r"], s["recip_r"][0:1, 8:16],
       incs=(("TN", 1),))
    P.wait_now("vector", "TN")
    V(P, "tensor_copy", out=s["gn_cols"], in_=ps[0][:, 0:16],
      incs=(("VE", 1),))
    P.barrier()


def l1_gn_oproj(P, cx, ps):
    s, d = cx.s, cx.d
    mu_b = s["P2c"][:, 0:2048]
    inv_b = s["P2c"][:, 2048:4096]
    yb0 = s["P2a"][:, 0:512]
    yb1 = s["P2a"][:, 512:1024]
    yn0 = s["P2a"][:, 1024:1536]
    yn1 = s["P2a"][:, 1536:2048]
    for tb in range(NTB):
        c0 = tb * 512
        srcs = [d["l1wo"][ich, och]
                for ich in range(NCH) for och in range(NCH)]
        state = {"ich": -1}
        gd_hist = {}
        P.wait("gpsimd", "VE", P.cnt.get("VE", 0))
        for pre in range(2):
            gd_hist[pre] = P.dma("gpsimd",
                                 (yb0, yb1)[pre % 2],
                                 d["y_scr"][pre][:, c0:c0 + 512])

        def consume(i, tile, c0=c0, state=state, gd_hist=gd_hist):
            ich, och = divmod(i, NCH)
            if ich != state["ich"]:
                state["ich"] = ich
                if ich > 0:
                    P.wait_now("vector", "TN")
                P.wait("vector", "GD", gd_hist[ich])
                yb = (yb0, yb1)[ich % 2]
                yn = (yn0, yn1)[ich % 2]
                V(P, "tensor_scalar", out=yn, in0=yb,
                  scalar1=s["gn_cols"][:, ich:ich + 1],
                  scalar2=s["gn_cols"][:, 8 + ich:9 + ich],
                  op0=OP.subtract, op1=OP.mult)
                V(P, "tensor_scalar", out=yn, in0=yn,
                  scalar1=s["gnw_col"][:, ich:ich + 1],
                  scalar2=s["gnb_col"][:, ich:ich + 1],
                  op0=OP.mult, op1=OP.add, incs=(("VE", 1),))
                if ich + 2 < NCH:
                    P.wait("gpsimd", "VE", P.cnt["VE"])
                    gd_hist[ich + 2] = P.dma(
                        "gpsimd", (yb0, yb1)[ich % 2],
                        d["y_scr"][ich + 2][:, c0:c0 + 512])
                P.wait_now("tensor", "VE")
            yn = (yn0, yn1)[ich % 2]
            MM(P, ps[och], tile, yn, start=(ich == 0), stop=(ich == NCH - 1),
               incs=(("TN", 1),), flush=False)

        stream_tiles(P, cx, srcs, consume)
        P.wait_now("vector", "TN")
        for och in range(NCH):
            V(P, "tensor_sub", out=s["evac"], in0=xs(cx, och, c0, c0 + 512),
              in1=mu_b[:, c0:c0 + 512])
            V(P, "tensor_mul", out=s["evac"], in0=s["evac"],
              in1=inv_b[:, c0:c0 + 512])
            V(P, "tensor_add", out=s["evac"], in0=s["evac"], in1=ps[och])
            V(P, "tensor_add", out=xs(cx, och, c0, c0 + 512),
              in0=xs(cx, och, c0, c0 + 512), in1=s["evac"],
              incs=(("VE", 1),))
            P.wait_now("tensor", "VE")
    P.barrier()


def l1_ffn_attn(P, cx, ps):
    s, d = cx.s, cx.d
    qblk = s["P2a"]
    kb0 = s["P2b"][:, 0:2048]
    kb1 = s["P2b"][:, 2048:4096]
    aT0 = s["P2d"][:, 512:1024]
    aT1 = s["P2d"][:, 1024:1536]
    vt = s["P2d"][:, 0:512]
    ao = s["P2c"]

    for qb in range(NTB):
        e = qb
        nch_e = NCH >> e
        c0 = qb * 512
        P.wait("sync", "TN", P.cnt.get("TN", 0))
        qd = []
        for och in range(NCH):
            qd.append(P.dma("sync", qblk[:, och * 512:(och + 1) * 512],
                            d["q_scr"][och][:, c0:c0 + 512]))
        kd = P.dma("sync", kb0, d["k_scr"][0])
        P.wait("tensor", "SD", qd[-1])
        tn_hd = []
        for h in range(RET_H):
            P.wait_now("tensor", "SC")
            kb = (kb0, kb1)[h % 2]
            if h + 1 < RET_H:
                if h >= 1:
                    P.wait("sync", "TN", tn_hd[h - 1])
                P.dma("sync", (kb0, kb1)[(h + 1) % 2], d["k_scr"][h + 1])
            P.wait("tensor", "SD", kd)
            P.wait("gpsimd", "TN", P.cnt.get("TN", 0))
            vg = []
            for kc in range(4):
                vg.append(P.dma("gpsimd", vt[:, kc * 128:(kc + 1) * 128],
                                d["v_scr"][kc][:, h * 128:(h + 1) * 128]))
            sc_after = []
            for kc in range(NTC):
                bank = ps[kc % 2]
                if kc >= 2:
                    P.wait("tensor", "SC", sc_after[kc - 2])
                MM(P, bank, kb[:, kc * 128:(kc + 1) * 128],
                   qblk[:, h * 512:(h + 1) * 512], incs=(("TN", 1),))
                P.wait_now("scalar", "TN")
                aT = aT0 if kc % 2 == 0 else aT1
                S(P, "activation", out=aT, in_=bank, func=AF.Exp,
                  bias=s["bias8"][:, h:h + 1], scale=SCL, incs=(("SC", 1),))
                sc_after.append(P.cnt["SC"])
                P.wait_now("tensor", "SC")
                P.wait("tensor", "GD", vg[kc])
                MM(P, ps[2], vt[:, (kc % 4) * 128:(kc % 4) * 128 + 128], aT,
                   start=(kc == 0), stop=(kc == NTC - 1))
                MM(P, ps[3][0:1, 0:512], s["ones_c"], aT,
                   start=(kc == 0), stop=(kc == NTC - 1),
                   incs=(("TN", 1),))
                if kc + 4 < NTC:
                    P.wait("gpsimd", "TN", P.cnt["TN"])
                    vg.append(P.dma("gpsimd",
                                    vt[:, (kc % 4) * 128:(kc % 4) * 128 + 128],
                                    d["v_scr"][kc + 4][:, h * 128:(h + 1) * 128]))
            kd = P.cnt["SD"]
            tn_hd.append(P.cnt["TN"])
            P.wait_now("vector", "TN")
            V(P, "tensor_copy", out=s["evac"], in_=ps[2])
            V(P, "reciprocal", out=s["recip_r"], in_=ps[3][0:1, 0:512],
              incs=(("VE", 1),))
            P.wait_now("tensor", "VE")
            MM(P, ps[4], s["ones_r"], s["recip_r"], incs=(("TN", 1),))
            P.wait_now("vector", "TN")
            V(P, "tensor_mul", out=ao[:, h * 512:(h + 1) * 512],
              in0=s["evac"], in1=ps[4], incs=(("VE", 1),))
            P.wait_now("tensor", "VE")
        srcs = [d["fo_t"][ich, och]
                for ich in range(nch_e) for och in range(nch_e)]

        def consume(i, tile, nch_e=nch_e):
            ich, och = divmod(i, nch_e)
            MM(P, ps[och], tile, ao[:, ich * 512:(ich + 1) * 512],
               start=(ich == 0), stop=(ich == nch_e - 1), incs=(("TN", 1),),
               flush=False)

        stream_tiles(P, cx, srcs, consume)
        P.wait_now("vector", "TN")
        for och in range(nch_e):
            V(P, "tensor_scalar_add", out=s["evac"], in0=ps[och],
              scalar1=s["fob_col"][:, och:och + 1])
            V(P, "tensor_add", out=xs(cx, och, c0, c0 + 512),
              in0=xs(cx, och, c0, c0 + 512), in1=s["evac"],
              incs=(("VE", 1),))
            P.wait_now("tensor", "VE")
    P.barrier()


STAGES = ["consts", "pass1", "derive1", "pass2", "stats", "xq", "attn",
          "ffn", "l1stats", "l1qkv", "ret", "gno", "l1attn"]


def build(upto="all"):
    nc = bass.Bass("TRN2", target_bir_lowering=False,
                   detect_race_conditions=False)
    cx = declare(nc)
    P = Prog(nc)
    for nm in ("SD", "GD", "CD", "VD", "TN", "VE", "SC",
               "bar_sync", "bar_tensor", "bar_vector", "bar_scalar",
               "bar_gpsimd"):
        P.sem(nm)
    nc.all_engine_barrier()
    nums = sorted(h.num for h in P.sems.values())
    assert nums == list(range(nums[0], nums[0] + len(nums))), nums
    clear_range = range(nums[0], nums[-1] + 1)
    nc.gpsimd.dma_reset(clear_range)
    nc.gpsimd.sem_clear(clear_range)
    nc.all_engine_barrier()
    load_consts(P, cx)

    def go(name):
        return upto == "all" or STAGES.index(name) <= STAGES.index(upto)

    for seq in range(SEQS):
        l0_load_x(P, cx, seq)
        P.barrier()
        if go("stats"):
            l0_stats(P, cx, cx.ps)
        elif go("pass2"):
            l0_stats(P, cx, cx.ps, sub="pass2")
        elif go("derive1"):
            l0_stats(P, cx, cx.ps, sub="derive1")
        elif go("pass1"):
            ln_stats_pass(P, cx, 0, 1, cx.ps, mode="plain")
            P.barrier()
        if go("xq"):
            l0_xq(P, cx, cx.ps)
        if go("attn"):
            l0_attn(P, cx, cx.ps)
        if go("ffn"):
            l0_ffn(P, cx, cx.ps, first_seq=(seq == 0))
        if go("l1stats"):
            l1_stats(P, cx, cx.ps)
        if go("l1qkv"):
            l1_ffn_qkv(P, cx, cx.ps)
        if go("ret"):
            l1_retention(P, cx, cx.ps, seq)
        if go("gno"):
            l1_gn_oproj(P, cx, cx.ps)
        if go("l1attn"):
            l1_ffn_attn(P, cx, cx.ps)
        P.barrier()
        for ich in range(NCH):
            P.dma("gpsimd", cx.d["out"][seq, ich], xs(cx, ich))
        P.wait_now("gpsimd", "GD")
        P.barrier()
    for i, (nm, npart, c0, c1) in enumerate(DBG_SBUF):
        dbg = nc.dram_tensor(f"dbg{i}", [npart, c1 - c0], F32,
                             kind="ExternalOutput").ap()
        P.dma("gpsimd", dbg, cx.s[nm][0:npart, c0:c1])
    P.wait_now("gpsimd", "GD")

    with nc.Block() as block:
        P.finalize(block)
    return nc


def kernel(**inputs):
    maps = host_inputs(inputs)
    nc = build(UPTO)
    res = run_bass_kernel_spmd(nc, maps, core_ids=list(range(NCORES)))
    outs = [np.asarray(r["out"]) for r in res.results]
    x = np.stack(outs)
    x = x.reshape(NCORES * SEQS, D, T).transpose(0, 2, 1)
    return np.ascontiguousarray(x).astype(np.float32)
